# version 03
# baseline (speedup 1.0000x reference)
import numpy as np
import ml_dtypes
from contextlib import ExitStack

import concourse.bass as bass
import concourse.mybir as mybir
from concourse.bass_utils import run_bass_kernel_spmd

F32 = mybir.dt.float32
BF16 = mybir.dt.bfloat16
I32 = mybir.dt.int32
AF = mybir.ActivationFunctionType
ALU = mybir.AluOpType

D = 2048
T = 2048
NF = 16384
H1 = 5632
DS = 64

ENGINES = ("sync", "gpsimd", "vector", "scalar", "tensor")

C_ZERO, C_ONE, C_EPS, C_75, C_KEPS, C_EPS8, C_HALF, C_X = range(8)


class Prog:

    def __init__(self, nc):
        self.nc = nc
        self.ops = {e: [] for e in ENGINES}
        self.semctr = {}
        self.marks = {}
        self.autosem = {}

    def on(self, eng, fn):
        self.ops[eng].append(fn)

    def _bump(self, sem, n):
        self.semctr[sem] = self.semctr.get(sem, 0) + n
        return self.semctr[sem]

    def val(self, sem):
        return self.semctr.get(sem, 0)

    def op(self, eng, fn, sem=None, n=1):
        auto = self.autosem.get(eng) if hasattr(self, "autosem") else None
        if auto is not None:
            v = self.val(auto)
            if v > 0:
                self.on(eng, lambda E, s=auto, v=v: E.wait_ge(s, v))
            if sem is None:
                sem, n = auto, 1
            else:
                assert sem is auto, (eng, "non-default sem on auto-synced engine")
        if sem is None:
            self.on(eng, fn)
            return None
        tgt = self._bump(sem, n)
        self.on(eng, lambda E, f=fn, s=sem, n=n: f(E).then_inc(s, n))
        return tgt

    def wait(self, eng, sem, v=None):
        if v is None:
            v = self.val(sem)
        if v > 0:
            self.on(eng, lambda E, s=sem, v=v: E.wait_ge(s, v))

    def dma(self, eng, out, in_, sem, **kw):
        return self.op(eng, lambda E, o=out, i=in_, kw=kw:
                       E.dma_start(out=o, in_=i, **kw), sem, 16)

    def run(self, block):
        for e in ENGINES:
            stream = self.ops[e]
            if stream:
                getattr(block, e)(lambda E, s=stream: [f(E) for f in s])


def _mk_nc():
    return bass.Bass("TRN2", target_bir_lowering=False, debug=False)


def _sems(es, nc, names):
    return {n: es.enter_context(nc.semaphore(n)) for n in names}



def build_ln(n_parts, out_x=False):
    nc = _mk_nc()
    xs_h = [nc.dram_tensor(f"xp{i}", [D, T], F32, kind="ExternalInput").ap()
            for i in range(n_parts)]
    consts = nc.dram_tensor("consts", [128, 8], F32, kind="ExternalInput").ap()
    onesc_h = nc.dram_tensor("onesc", [128, 1], F32, kind="ExternalInput").ap()
    onesr_h = nc.dram_tensor("onesr", [1, 128], F32, kind="ExternalInput").ap()
    xn_h = nc.dram_tensor("xn", [D, T], BF16, kind="ExternalOutput").ap()
    xsum_h = (nc.dram_tensor("xsum", [D, T], F32, kind="ExternalOutput").ap()
              if out_x else None)
    es = ExitStack()
    with es:
        xs = es.enter_context(nc.sbuf_tensor("xs", [128, 16 * T], F32))
        xtmp = es.enter_context(nc.sbuf_tensor("xtmp", [128, T], F32))
        sq = es.enter_context(nc.sbuf_tensor("sq", [128, T], F32))
        ct = es.enter_context(nc.sbuf_tensor("ct", [128, 8], F32))
        onesc = es.enter_context(nc.sbuf_tensor("s_onesc", [128, 1], F32))
        onesr = es.enter_context(nc.sbuf_tensor("s_onesr", [1, 128], F32))
        srow = es.enter_context(nc.sbuf_tensor("srow", [1, 4 * T], F32))
        bc = es.enter_context(nc.sbuf_tensor("bc", [128, 2 * T], F32))
        xnb = es.enter_context(nc.sbuf_tensor("xnb", [128, 2 * T], BF16))
        ps = es.enter_context(nc.psum_tensor("ps", [1, T], F32))
        psb = es.enter_context(nc.psum_tensor("psb", [128, 512], F32))
        S = _sems(es, nc, ["sd", "sg", "so", "so1", "sv", "ss", "sm"])
        sd, sg, so, so1, sv, ss, sm = (S[k] for k in
                                       ["sd", "sg", "so", "so1", "sv", "ss", "sm"])
        soP = [so, so1]
        p = Prog(nc)
        p.autosem = {"vector": sv, "scalar": ss}

        p.dma("sync", ct[:, :], consts, sd)
        p.dma("sync", onesc[:, :], onesc_h, sd)
        p.dma("sync", onesr[:, :], onesr_h, sd)
        for c in range(16):
            p.dma("sync", xs[:, c * T:(c + 1) * T], xs_h[0][c * 128:(c + 1) * 128, :], sd)
        for ip in range(1, n_parts):
            for c in range(16):
                p.dma("sync", xtmp[:, :], xs_h[ip][c * 128:(c + 1) * 128, :], sd)
                p.wait("vector", sd)
                p.wait("vector", sg)
                p.op("vector", lambda E, c=c: E.tensor_tensor(
                    xs[:, c * T:(c + 1) * T], xs[:, c * T:(c + 1) * T],
                    xtmp[:, :], ALU.add), sv)
                p.wait("sync", sv)
        if out_x:
            p.wait("sync", sv)
            for c in range(16):
                p.dma("sync", xsum_h[c * 128:(c + 1) * 128, :],
                      xs[:, c * T:(c + 1) * T], so)
        p.wait("tensor", sd)
        p.wait("tensor", sg)
        p.wait("tensor", sv)
        for c in range(16):
            for tt in range(4):
                p.op("tensor", lambda E, c=c, tt=tt: E.matmul(
                    ps[0:1, tt * 512:(tt + 1) * 512], onesc[:, :],
                    xs[:, c * T + tt * 512:c * T + (tt + 1) * 512],
                    start=(c == 0), stop=(c == 15)),
                    sm if (c == 15 and tt == 3) else None)
        p.wait("vector", sm)
        MU, RS, NB, V1 = 0, 1, 2, 3
        p.op("vector", lambda E: E.tensor_scalar(
            srow[0:1, MU * T:(MU + 1) * T], ps[0:1, :], 1.0 / D, None, ALU.mult), sv)
        p.wait("scalar", sv)
        for c in range(16):
            p.wait("scalar", sm, p.val(sm))
            p.op("scalar", lambda E, c=c: E.activation(
                sq[:, :], xs[:, c * T:(c + 1) * T], AF.Square), ss)
            p.wait("tensor", ss)
            for tt in range(4):
                p.op("tensor", lambda E, c=c, tt=tt: E.matmul(
                    ps[0:1, tt * 512:(tt + 1) * 512], onesc[:, :],
                    sq[:, tt * 512:(tt + 1) * 512],
                    start=(c == 0), stop=(c == 15)),
                    sm if (tt == 3) else None)
        p.wait("vector", sm)
        p.op("vector", lambda E: E.tensor_scalar(
            srow[0:1, V1 * T:(V1 + 1) * T], ps[0:1, :], 1.0 / D, None, ALU.mult), sv)
        p.wait("vector", sv)
        p.op("vector", lambda E: E.tensor_tensor(
            srow[0:1, RS * T:(RS + 1) * T], srow[0:1, MU * T:(MU + 1) * T],
            srow[0:1, MU * T:(MU + 1) * T], ALU.mult), sv)
        p.wait("vector", sv)
        p.op("vector", lambda E: E.tensor_tensor(
            srow[0:1, V1 * T:(V1 + 1) * T], srow[0:1, V1 * T:(V1 + 1) * T],
            srow[0:1, RS * T:(RS + 1) * T], ALU.subtract), sv)
        p.wait("scalar", sv)
        p.op("scalar", lambda E: E.activation(
            srow[0:1, RS * T:(RS + 1) * T], srow[0:1, V1 * T:(V1 + 1) * T],
            AF.Ln, bias=ct[0:1, C_EPS:C_EPS + 1]), ss)
        p.wait("scalar", ss)
        p.op("scalar", lambda E: E.activation(
            srow[0:1, RS * T:(RS + 1) * T], srow[0:1, RS * T:(RS + 1) * T],
            AF.Exp, scale=-0.5), ss)
        p.wait("vector", ss)
        p.op("vector", lambda E: E.tensor_tensor(
            srow[0:1, NB * T:(NB + 1) * T], srow[0:1, MU * T:(MU + 1) * T],
            srow[0:1, RS * T:(RS + 1) * T], ALU.mult), sv)
        p.wait("vector", sv)
        p.op("vector", lambda E: E.tensor_scalar(
            srow[0:1, NB * T:(NB + 1) * T], srow[0:1, NB * T:(NB + 1) * T],
            -1.0, None, ALU.mult), sv)
        p.wait("vector", sv)
        p.wait("tensor", sv)
        for half, col in ((0, RS), (1, NB)):
            for tt in range(4):
                p.op("tensor", lambda E, col=col, tt=tt: E.matmul(
                    psb[:, :], onesr[:, :],
                    srow[0:1, col * T + tt * 512:col * T + (tt + 1) * 512],
                    start=True, stop=True), sm)
                p.wait("vector", sm)
                p.op("vector", lambda E, half=half, tt=tt: E.tensor_copy(
                    bc[:, half * T + tt * 512:half * T + (tt + 1) * 512],
                    psb[:, :]), sv)
                p.wait("tensor", sv)
        for c in range(16):
            buf = c % 2
            if c >= 2:
                p.wait("vector", soP[buf])
            p.op("vector", lambda E, c=c: E.tensor_tensor(
                sq[:, :], xs[:, c * T:(c + 1) * T], bc[:, 0:T], ALU.mult), sv)
            p.op("vector", lambda E, c=c, buf=buf: E.tensor_tensor(
                xnb[:, buf * T:(buf + 1) * T], sq[:, :], bc[:, T:2 * T], ALU.add), sv)
            p.wait("sync", sv)
            p.dma("sync", xn_h[c * 128:(c + 1) * 128, :],
                  xnb[:, buf * T:(buf + 1) * T], soP[buf])
        p.wait("sync", so)
        p.wait("sync", so1)
        p.wait("sync", sd)
        p.wait("sync", sg)
        block = es.enter_context(nc.Block())
        p.run(block)
    return nc



def _emit_qkv_chunk(p, tiles, ecs, blk_off, rope_pair, fuse, S):
    nc = p.nc
    sd, sg, sv, ss, sm, se = S["sd"], S["sg"], S["sv"], S["ss"], S["sm"], S["se"]
    xn, wb = tiles["xn"], tiles["wb"]
    winT = tiles["winT_h"]
    pss = [tiles["ps0"], tiles["ps1"]]
    t12 = [tiles["t1"], tiles["t2"]]
    binc = tiles["binc"]
    p.wait("gpsimd", sm)
    for j, ec in enumerate(ecs):
        col = blk_off + ec * 128
        p.dma("gpsimd", wb[:, j * 2048:(j + 1) * 2048],
              winT.rearrange("(dc k) e -> k dc e", k=128)[:, :, col:col + 128], sg)
    p.wait("tensor", sd)
    p.wait("tensor", sg)
    p.wait("tensor", sv, p.marks.get("qkv_psum_free", 0))
    for j, ec in enumerate(ecs):
        for tt in range(4):
            for dc in range(16):
                p.op("tensor", lambda E, j=j, dc=dc, tt=tt: E.matmul(
                    pss[j][:, tt * 512:(tt + 1) * 512],
                    wb[:, j * 2048 + dc * 128:j * 2048 + (dc + 1) * 128],
                    xn[:, dc * T + tt * 512:dc * T + (tt + 1) * 512],
                    start=(dc == 0), stop=(dc == 15)),
                    sm if (dc == 15 and tt == 3) else None)
    p.wait("scalar", sm)
    p.wait("scalar", sv)
    for j, ec in enumerate(ecs):
        bcol = (blk_off + ec * 128) // 128
        p.op("scalar", lambda E, j=j, bcol=bcol: E.activation(
            t12[j][:, :], pss[j][:, :], AF.Identity,
            bias=binc[:, bcol:bcol + 1]), ss)
    if rope_pair:
        cosb, sinb = tiles["cosb"], tiles["sinb"]
        r, u = tiles["rsc"], tiles["ru"]
        p.wait("vector", ss)
        p.op("vector", lambda E: E.tensor_tensor(r[:, 0:T], t12[1][:, :], sinb[:, :], ALU.mult), sv)
        p.op("vector", lambda E: E.tensor_tensor(u[:, 0:T], t12[0][:, :], sinb[:, :], ALU.mult), sv)
        p.op("vector", lambda E: E.tensor_tensor(t12[0][:, :], t12[0][:, :], cosb[:, :], ALU.mult), sv)
        p.op("vector", lambda E: E.tensor_tensor(t12[1][:, :], t12[1][:, :], cosb[:, :], ALU.mult), sv)
        p.op("vector", lambda E: E.tensor_tensor(t12[0][:, :], t12[0][:, :], r[:, 0:T], ALU.subtract), sv)
        p.op("vector", lambda E: E.tensor_tensor(t12[1][:, :], t12[1][:, :], u[:, 0:T], ALU.add), sv)
    fuse(p)
    p.marks["qkv_psum_free"] = p.val(sv)


def build_kva():
    nc = _mk_nc()
    xn_h = nc.dram_tensor("xn", [D, T], BF16, kind="ExternalInput").ap()
    winT_h = nc.dram_tensor("winT", [D, 3 * D], F32, kind="ExternalInput").ap()
    binc_h = nc.dram_tensor("binc", [128, 48], F32, kind="ExternalInput").ap()
    cos_h = nc.dram_tensor("cosr", [512, T], F32, kind="ExternalInput").ap()
    sin_h = nc.dram_tensor("sinr", [512, T], F32, kind="ExternalInput").ap()
    consts = nc.dram_tensor("consts", [128, 8], F32, kind="ExternalInput").ap()
    onesc_h = nc.dram_tensor("onesc", [128, 1], F32, kind="ExternalInput").ap()
    onesr_h = nc.dram_tensor("onesr", [1, 128], F32, kind="ExternalInput").ap()
    ap_h = nc.dram_tensor("apart", [128, 16], F32, kind="ExternalOutput").ap()
    es = ExitStack()
    with es:
        xn = es.enter_context(nc.sbuf_tensor("s_xn", [128, 16 * T], BF16))
        kt = es.enter_context(nc.sbuf_tensor("kt", [128, 16 * T], BF16))
        wb = es.enter_context(nc.sbuf_tensor("wb", [128, 2 * 2048], BF16))
        t1 = es.enter_context(nc.sbuf_tensor("t1", [128, T], F32))
        t2 = es.enter_context(nc.sbuf_tensor("t2", [128, T], F32))
        rsc = es.enter_context(nc.sbuf_tensor("rsc", [128, T], F32))
        cosb = es.enter_context(nc.sbuf_tensor("cosb", [128, T], F32))
        sinb = es.enter_context(nc.sbuf_tensor("sinb", [128, T], F32))
        binc = es.enter_context(nc.sbuf_tensor("s_binc", [128, 48], F32))
        ct = es.enter_context(nc.sbuf_tensor("ct", [128, 8], F32))
        onesc = es.enter_context(nc.sbuf_tensor("s_onesc", [128, 1], F32))
        onesr = es.enter_context(nc.sbuf_tensor("s_onesr", [1, 128], F32))
        srow = es.enter_context(nc.sbuf_tensor("srow", [1, T], F32))
        rnb = es.enter_context(nc.sbuf_tensor("rnb", [128, T], F32))
        apt = es.enter_context(nc.sbuf_tensor("apt", [128, 16], F32))
        sq = es.enter_context(nc.sbuf_tensor("sq", [128, T], F32))
        ps0 = es.enter_context(nc.psum_tensor("ps0", [128, T], F32))
        ps1 = es.enter_context(nc.psum_tensor("ps1", [128, T], F32))
        S = _sems(es, nc, ["sd", "sg", "so", "sv", "ss", "sm", "se"])
        sd, sg, so, sv, ss, sm, se = (S[k] for k in ["sd", "sg", "so", "sv", "ss", "sm", "se"])
        p = Prog(nc)
        p.autosem = {"vector": sv, "scalar": ss}
        tiles = dict(xn=xn, wb=wb, ps0=ps0, ps1=ps1, t1=t1, t2=t2, rsc=rsc,
                     ru=sq, cosb=cosb, sinb=sinb, binc=binc, winT_h=winT_h)

        p.dma("sync", binc[:, :], binc_h, sd)
        p.dma("sync", ct[:, :], consts, sd)
        p.dma("sync", onesc[:, :], onesc_h, sd)
        p.dma("sync", onesr[:, :], onesr_h, sd)
        for c in range(16):
            p.dma("sync", xn[:, c * T:(c + 1) * T], xn_h[c * 128:(c + 1) * 128, :], sd)
        for pg in range(4):
            p.wait("sync", sv)
            p.dma("sync", cosb[:, :], cos_h[pg * 128:(pg + 1) * 128, :], sd)
            p.dma("sync", sinb[:, :], sin_h[pg * 128:(pg + 1) * 128, :], sd)
            p.wait("vector", sd)
            p.wait("vector", sg)

            def fuse(p, pg=pg):
                p.wait("vector", S["ss"])
                p.op("vector", lambda E, pg=pg: E.tensor_copy(
                    kt[:, pg * T:(pg + 1) * T], t1[:, :]), sv, 1)
                p.op("vector", lambda E, pg=pg: E.tensor_copy(
                    kt[:, (pg + 4) * T:(pg + 5) * T], t2[:, :]), sv, 1)
            _emit_qkv_chunk(p, tiles, [pg, pg + 4], D, True, fuse, S)
        for kc in range(8, 16):
            def fuse(p, kc=kc):
                p.wait("vector", S["ss"])
                p.op("vector", lambda E, kc=kc: E.tensor_copy(
                    kt[:, kc * T:(kc + 1) * T], t1[:, :]), sv, 1)
            _emit_qkv_chunk(p, tiles, [kc], D, False, fuse, S)
        p.wait("scalar", sv)
        for c in range(16):
            p.wait("scalar", sm, p.val(sm))
            p.op("scalar", lambda E, c=c: E.activation(
                sq[:, :], kt[:, c * T:(c + 1) * T], AF.Square), ss)
            p.wait("tensor", ss)
            for tt in range(4):
                p.op("tensor", lambda E, c=c, tt=tt: E.matmul(
                    ps0[0:1, tt * 512:(tt + 1) * 512], onesc[:, :],
                    sq[:, tt * 512:(tt + 1) * 512],
                    start=(c == 0), stop=(c == 15)), sm if tt == 3 else None)
        p.wait("vector", sm)
        p.op("vector", lambda E: E.tensor_scalar(
            srow[0:1, :], ps0[0:1, :], 1e-24, None, ALU.max), sv)
        p.wait("scalar", sv)
        p.op("scalar", lambda E: E.activation(srow[0:1, :], srow[0:1, :], AF.Ln), ss)
        p.wait("scalar", ss)
        p.op("scalar", lambda E: E.activation(srow[0:1, :], srow[0:1, :], AF.Exp, scale=-0.5), ss)
        p.wait("tensor", ss)
        for tt in range(4):
            p.op("tensor", lambda E, tt=tt: E.matmul(
                ps1[:, tt * 512:(tt + 1) * 512], onesr[:, :],
                srow[0:1, tt * 512:(tt + 1) * 512], start=True, stop=True),
                sm if tt == 3 else None)
        p.wait("vector", sm)
        p.op("vector", lambda E: E.tensor_copy(rnb[:, :], ps1[:, :]), sv)
        p.marks["qkv_psum_free"] = p.val(sv)
        for ec in range(16):
            def fuse(p, ec=ec):
                p.wait("vector", S["ss"])
                p.op("vector", lambda E, ec=ec: E.tensor_tensor(
                    rsc[:, 0:T], kt[:, ec * T:(ec + 1) * T], t1[:, :], ALU.mult), sv)
                p.op("vector", lambda E: E.tensor_tensor(
                    rsc[:, 0:T], rsc[:, 0:T], rnb[:, :], ALU.mult), sv)
                p.op("vector", lambda E, ec=ec: E.reduce_sum(
                    apt[:, ec:ec + 1], rsc[:, 0:T], mybir.AxisListType.X), sv)
            _emit_qkv_chunk(p, tiles, [ec], 2 * D, False, fuse, S)
        p.wait("sync", sv)
        p.dma("sync", ap_h, apt[:, :], so)
        p.wait("sync", so)
        p.wait("sync", sd)
        p.wait("sync", sg)
        block = es.enter_context(nc.Block())
        p.run(block)
    return nc



def build_qattn():
    nc = _mk_nc()
    xn_h = nc.dram_tensor("xn", [D, T], BF16, kind="ExternalInput").ap()
    x0_h = nc.dram_tensor("x0", [D, T], F32, kind="ExternalInput").ap()
    winT_h = nc.dram_tensor("winT", [D, 3 * D], F32, kind="ExternalInput").ap()
    binc_h = nc.dram_tensor("binc", [128, 48], F32, kind="ExternalInput").ap()
    cos_h = nc.dram_tensor("cosr", [512, T], F32, kind="ExternalInput").ap()
    sin_h = nc.dram_tensor("sinr", [512, T], F32, kind="ExternalInput").ap()
    woutT_h = nc.dram_tensor("woutT", [D, D], F32, kind="ExternalInput").ap()
    boutc_h = nc.dram_tensor("boutc", [128, 16], F32, kind="ExternalInput").ap()
    am_h = nc.dram_tensor("amine", [128, 16], F32, kind="ExternalInput").ap()
    ap2_h = nc.dram_tensor("apartner", [128, 16], F32, kind="ExternalInput").ap()
    px1_h = nc.dram_tensor("px1", [D, T], F32, kind="ExternalOutput").ap()
    es = ExitStack()
    with es:
        xn = es.enter_context(nc.sbuf_tensor("s_xn", [128, 16 * T], BF16))
        yt = es.enter_context(nc.sbuf_tensor("yt", [128, 16 * T], BF16))
        wb = es.enter_context(nc.sbuf_tensor("wb", [128, 2 * 2048], BF16))
        wob = es.enter_context(nc.sbuf_tensor("wob", [128, 2048], BF16))
        t1 = es.enter_context(nc.sbuf_tensor("t1", [128, T], F32))
        t2 = es.enter_context(nc.sbuf_tensor("t2", [128, T], F32))
        rsc = es.enter_context(nc.sbuf_tensor("rsc", [128, T], F32))
        ru = es.enter_context(nc.sbuf_tensor("ru", [128, T], F32))
        cosb = es.enter_context(nc.sbuf_tensor("cosb", [128, T], F32))
        sinb = es.enter_context(nc.sbuf_tensor("sinb", [128, T], F32))
        binc = es.enter_context(nc.sbuf_tensor("s_binc", [128, 48], F32))
        boutc = es.enter_context(nc.sbuf_tensor("s_boutc", [128, 16], F32))
        at_ = es.enter_context(nc.sbuf_tensor("at_", [128, 16], F32))
        ap2 = es.enter_context(nc.sbuf_tensor("ap2", [128, 16], F32))
        x0b = es.enter_context(nc.sbuf_tensor("x0b", [128, T], F32))
        pxb = es.enter_context(nc.sbuf_tensor("pxb", [128, T], F32))
        sq = es.enter_context(nc.sbuf_tensor("sq", [128, 1], F32))
        ps0 = es.enter_context(nc.psum_tensor("ps0", [128, T], F32))
        ps1 = es.enter_context(nc.psum_tensor("ps1", [128, T], F32))
        S = _sems(es, nc, ["sd", "sg", "so", "sv", "ss", "sm", "se"])
        sd, sg, so, sv, ss, sm, se = (S[k] for k in ["sd", "sg", "so", "sv", "ss", "sm", "se"])
        p = Prog(nc)
        p.autosem = {"vector": sv, "scalar": ss}
        tiles = dict(xn=xn, wb=wb, ps0=ps0, ps1=ps1, t1=t1, t2=t2, rsc=rsc,
                     ru=ru, cosb=cosb, sinb=sinb, binc=binc, winT_h=winT_h)

        p.dma("sync", binc[:, :], binc_h, sd)
        p.dma("sync", boutc[:, :], boutc_h, sd)
        p.dma("sync", at_[:, :], am_h, sd)
        p.dma("sync", ap2[:, :], ap2_h, sd)
        for c in range(16):
            p.dma("sync", xn[:, c * T:(c + 1) * T], xn_h[c * 128:(c + 1) * 128, :], sd)
        p.wait("vector", sd)
        p.wait("vector", sg)
        p.op("vector", lambda E: E.tensor_tensor(at_[:, :], at_[:, :], ap2[:, :], ALU.add), sv)
        p.wait("vector", sv)

        def mk_fuse(chunks):
            def fuse(p):
                p.wait("vector", S["ss"])
                for j, qc in enumerate(chunks):
                    p.op("vector", lambda E, j=j, qc=qc: E.tensor_scalar(
                        yt[:, qc * T:(qc + 1) * T], (t1 if j == 0 else t2)[:, :],
                        at_[:, qc:qc + 1], None, ALU.mult), sv)
            return fuse
        for pg in range(4):
            p.wait("sync", sv)
            p.dma("sync", cosb[:, :], cos_h[pg * 128:(pg + 1) * 128, :], sd)
            p.dma("sync", sinb[:, :], sin_h[pg * 128:(pg + 1) * 128, :], sd)
            p.wait("vector", sd)
            p.wait("vector", sg)
            _emit_qkv_chunk(p, tiles, [pg, pg + 4], 0, True, mk_fuse([pg, pg + 4]), S)
        for qc in range(8, 16):
            _emit_qkv_chunk(p, tiles, [qc], 0, False, mk_fuse([qc]), S)

        for dc in range(16):
            p.wait("gpsimd", sm)
            p.dma("gpsimd", wob[:, :],
                  woutT_h.rearrange("(ec k) d -> k ec d", k=128)[:, :, dc * 128:(dc + 1) * 128], sg)
            p.wait("sync", sv, p.val(sv))
            p.dma("sync", x0b[:, :], x0_h[dc * 128:(dc + 1) * 128, :], sd)
            p.wait("tensor", sd)
            p.wait("tensor", sg)
            p.wait("tensor", ss, p.marks.get("qo_psum_free", 0))
            for tt in range(4):
                for ec in range(16):
                    p.op("tensor", lambda E, dc=dc, tt=tt, ec=ec: E.matmul(
                        ps0[:, tt * 512:(tt + 1) * 512],
                        wob[:, ec * 128:(ec + 1) * 128],
                        yt[:, ec * T + tt * 512:ec * T + (tt + 1) * 512],
                        start=(ec == 0), stop=(ec == 15)),
                        sm if (ec == 15 and tt == 3) else None)
            p.wait("scalar", sm)
            p.wait("scalar", sv)
            p.op("scalar", lambda E, dc=dc: E.activation(
                t1[:, :], ps0[:, :], AF.Identity, bias=boutc[:, dc:dc + 1]), ss)
            p.marks["qo_psum_free"] = p.val(ss)
            p.wait("vector", ss)
            p.op("vector", lambda E, dc=dc: E.tensor_tensor(
                t2[:, :], t1[:, :], xn[:, dc * T:(dc + 1) * T], ALU.add), sv)
            if dc >= 1:
                p.wait("vector", so, 16 * dc)
            p.op("vector", lambda E, dc=dc: E.tensor_tensor(
                pxb[:, :], t2[:, :], x0b[:, :], ALU.add), sv)
            p.wait("sync", sv)
            p.dma("sync", px1_h[dc * 128:(dc + 1) * 128, :], pxb[:, :], so)
        p.wait("sync", so)
        p.wait("sync", sd)
        p.wait("sync", sg)
        block = es.enter_context(nc.Block())
        p.run(block)
    return nc



def emit_decode(p, S, src, scl, out_lo, out_hi, scr, ct, half=False):
    sv, ss = S["sv"], S["ss"]
    i4l, i4h = scr["i4l"], scr["i4h"]
    sgl, sgh, q1, q2 = scr["sgl"], scr["sgh"], scr["q1"], scr["q2"]
    b = 0.5 if half else 1.0
    p.op("vector", lambda E: E.tensor_scalar(i4l, src, 15, None, ALU.bitwise_and), sv)
    p.op("vector", lambda E: E.tensor_scalar(i4h, src, 4, None, ALU.logical_shift_right), sv)
    p.op("vector", lambda E: E.tensor_scalar(sgl, i4l, 8, None, ALU.is_lt), sv)
    p.op("vector", lambda E: E.tensor_scalar(sgl, sgl, 2.0, -1.0, ALU.mult, ALU.add), sv)
    p.op("vector", lambda E: E.tensor_scalar(sgh, i4h, 8, None, ALU.is_lt), sv)
    p.op("vector", lambda E: E.tensor_scalar(sgh, sgh, 2.0, -1.0, ALU.mult, ALU.add), sv)
    for j4, sg, out in ((i4l, sgl, out_lo), (i4h, sgh, out_hi)):
        p.op("vector", lambda E, j4=j4: E.tensor_scalar(j4, j4, 7, None, ALU.bitwise_and), sv)
        p.op("vector", lambda E, j4=j4: E.tensor_scalar(q1, j4, 0.5 * b, None, ALU.mult), sv)
        p.op("vector", lambda E, j4=j4: E.tensor_scalar(q2, j4, b, -2.0 * b, ALU.mult, ALU.add), sv)
        p.op("vector", lambda E: E.tensor_tensor(q1, q1, q2, ALU.max), sv)
        p.op("vector", lambda E, j4=j4: E.tensor_scalar(q2, j4, 2.0 * b, -8.0 * b, ALU.mult, ALU.add), sv)
        p.op("vector", lambda E: E.tensor_tensor(q1, q1, q2, ALU.max), sv)
        p.op("vector", lambda E, sg=sg: E.tensor_tensor(q1, q1, sg, ALU.mult), sv)
        p.op("vector", lambda E, out=out: E.tensor_tensor(out, q1, scl, ALU.mult), sv)


def _decode_scratch(es, nc, W, tag=""):
    return dict(
        i4l=es.enter_context(nc.sbuf_tensor(f"i4l{tag}", [128, W], I32))[:, :],
        i4h=es.enter_context(nc.sbuf_tensor(f"i4h{tag}", [128, W], I32))[:, :],
        sgl=es.enter_context(nc.sbuf_tensor(f"sgl{tag}", [128, W], BF16))[:, :],
        sgh=es.enter_context(nc.sbuf_tensor(f"sgh{tag}", [128, W], BF16))[:, :],
        q1=es.enter_context(nc.sbuf_tensor(f"q1{tag}", [128, W], BF16))[:, :],
        q2=es.enter_context(nc.sbuf_tensor(f"q2{tag}", [128, W], BF16))[:, :],
    )


def _scr_view(scr, W):
    return {k: v[:, 0:W] for k, v in scr.items()}



def build_ffn0a():
    nc = _mk_nc()
    xn_h = nc.dram_tensor("xn", [D, T], BF16, kind="ExternalInput").ap()
    gpk_h = nc.dram_tensor("gpk", [1024, NF], I32, kind="ExternalInput").ap()
    gsc_h = nc.dram_tensor("gsc", [64, NF], F32, kind="ExternalInput").ap()
    upk_h = nc.dram_tensor("upk", [1024, NF], I32, kind="ExternalInput").ap()
    usc_h = nc.dram_tensor("usc", [64, NF], F32, kind="ExternalInput").ap()
    dpk_h = nc.dram_tensor("dpk", [NF // 2, D], I32, kind="ExternalInput").ap()
    dsc_h = nc.dram_tensor("dsc", [512, D], F32, kind="ExternalInput").ap()
    consts = nc.dram_tensor("consts", [128, 8], F32, kind="ExternalInput").ap()
    ddec_h = nc.dram_tensor("ddec", [NF, D], BF16, kind="ExternalOutput").ap()
    inter_h = nc.dram_tensor("inter", [NF, T], BF16, kind="ExternalOutput").ap()
    es = ExitStack()
    with es:
        xn = es.enter_context(nc.sbuf_tensor("s_xn", [128, 16 * T], BF16))
        ct = es.enter_context(nc.sbuf_tensor("ct", [128, 8], F32))
        gpkb = es.enter_context(nc.sbuf_tensor("gpkb", [128, 4096], I32))
        upkb = es.enter_context(nc.sbuf_tensor("upkb", [128, 4096], I32))
        gscb = es.enter_context(nc.sbuf_tensor("gscb", [128, 4096], BF16))
        uscb = es.enter_context(nc.sbuf_tensor("uscb", [128, 4096], BF16))
        glo = es.enter_context(nc.sbuf_tensor("glo", [128, 4096], BF16))
        ghi = es.enter_context(nc.sbuf_tensor("ghi", [128, 4096], BF16))
        ulo = es.enter_context(nc.sbuf_tensor("ulo", [128, 4096], BF16))
        uhi = es.enter_context(nc.sbuf_tensor("uhi", [128, 4096], BF16))
        scr = _decode_scratch(es, nc, 2048)
        eg = es.enter_context(nc.sbuf_tensor("eg", [128, 512], F32))
        e2 = es.enter_context(nc.sbuf_tensor("e2", [128, 512], F32))
        g3 = es.enter_context(nc.sbuf_tensor("g3", [128, 512], F32))
        th = es.enter_context(nc.sbuf_tensor("th", [128, 512], F32))
        tp1 = es.enter_context(nc.sbuf_tensor("tp1", [128, 512], F32))
        ue = es.enter_context(nc.sbuf_tensor("ue", [128, 512], BF16))
        intb = es.enter_context(nc.sbuf_tensor("intb", [128, 2 * T], BF16))
        psg = es.enter_context(nc.psum_tensor("psg", [128, T], F32))
        psu = es.enter_context(nc.psum_tensor("psu", [128, T], F32))
        S = _sems(es, nc, ["sd", "sg", "so", "sv", "ss", "sm", "se",
                           "sk0", "sk1", "sc0", "sc1", "st0", "st1", "si0", "si1"])
        (sd, sg, so, sv, ss, sm, se, sk0, sk1, sc0, sc1, st0, st1, si0, si1) = (
            S[k] for k in ["sd", "sg", "so", "sv", "ss", "sm", "se",
                           "sk0", "sk1", "sc0", "sc1", "st0", "st1", "si0", "si1"])
        skP, scP, stP, siP = [sk0, sk1], [sc0, sc1], [st0, st1], [si0, si1]
        p = Prog(nc)
        p.autosem = {"vector": sv, "scalar": ss}

        p.dma("sync", ct[:, :], consts, sd)
        for c in range(16):
            p.dma("sync", xn[:, c * T:(c + 1) * T], xn_h[c * 128:(c + 1) * 128, :], sd)

        pkb, sclb, olo, ohi = gpkb, gscb, glo, ghi
        for c in range(64):
            hb = c % 2
            if c >= 2:
                p.wait("sync", sv, p.val(sv))
                p.wait("gpsimd", sv, p.val(sv))
            p.dma("sync", pkb[:, hb * D:(hb + 1) * D],
                  dpk_h[c * 128:(c + 1) * 128, :], skP[hb])
            p.dma("gpsimd", sclb[:, hb * D:(hb + 1) * D],
                  dsc_h[c * 8:(c + 1) * 8, :].unsqueeze(1).broadcast_to([8, 16, D]), scP[hb])
            p.wait("vector", sd)
            p.wait("vector", skP[hb])
            p.wait("vector", scP[hb])
            if c >= 2:
                p.wait("vector", stP[hb])
            emit_decode(p, S, pkb[:, hb * D:(hb + 1) * D], sclb[:, hb * D:(hb + 1) * D],
                        olo[:, hb * D:(hb + 1) * D], ohi[:, hb * D:(hb + 1) * D],
                        scr, ct, half=True)
            p.wait("sync", sv)
            p.dma("sync", ddec_h[c * 128:(c + 1) * 128, :], olo[:, hb * D:(hb + 1) * D], stP[hb])
            p.dma("sync", ddec_h[NF // 2 + c * 128:NF // 2 + (c + 1) * 128, :],
                  ohi[:, hb * D:(hb + 1) * D], stP[hb])

        p.wait("sync", sv)
        p.wait("gpsimd", sv)
        p.wait("vector", st0)
        p.wait("vector", st1)
        for ns in range(32):
            n0 = ns * 512
            p.wait("sync", sm)
            p.wait("gpsimd", sm)
            p.dma("sync", gpkb[:, :],
                  gpk_h.rearrange("(kc k) n -> k kc n", k=128)[:, :, n0:n0 + 512], sd)
            p.dma("sync", upkb[:, :],
                  upk_h.rearrange("(kc k) n -> k kc n", k=128)[:, :, n0:n0 + 512], sd)
            for kc in range(8):
                p.dma("gpsimd", gscb[:, kc * 512:(kc + 1) * 512],
                      gsc_h[kc * 8:(kc + 1) * 8, n0:n0 + 512]
                      .unsqueeze(1).broadcast_to([8, 16, 512]), sg)
                p.dma("gpsimd", uscb[:, kc * 512:(kc + 1) * 512],
                      usc_h[kc * 8:(kc + 1) * 8, n0:n0 + 512]
                      .unsqueeze(1).broadcast_to([8, 16, 512]), sg)
            p.wait("vector", sd)
            p.wait("vector", sg)
            p.wait("vector", sm)
            for half_ in range(2):
                W0 = half_ * 2048
                emit_decode(p, S, gpkb[:, W0:W0 + 2048], gscb[:, W0:W0 + 2048],
                            glo[:, W0:W0 + 2048], ghi[:, W0:W0 + 2048], scr, ct)
                emit_decode(p, S, upkb[:, W0:W0 + 2048], uscb[:, W0:W0 + 2048],
                            ulo[:, W0:W0 + 2048], uhi[:, W0:W0 + 2048], scr, ct)
            p.wait("tensor", sv)
            for nsub in range(4):
                nc0 = nsub * 128
                p.wait("tensor", sv, p.marks.get("gu_psum_free", 0))
                for ps_, lo, hi in ((psg, glo, ghi), (psu, ulo, uhi)):
                    for tt in range(4):
                        for kc in range(8):
                            p.op("tensor", lambda E, ps_=ps_, lo=lo, tt=tt, kc=kc, nc0=nc0: E.matmul(
                                ps_[:, tt * 512:(tt + 1) * 512],
                                lo[:, kc * 512 + nc0:kc * 512 + nc0 + 128],
                                xn[:, kc * T + tt * 512:kc * T + (tt + 1) * 512],
                                start=(kc == 0), stop=False))
                        for kc in range(8):
                            p.op("tensor", lambda E, ps_=ps_, hi=hi, tt=tt, kc=kc, nc0=nc0: E.matmul(
                                ps_[:, tt * 512:(tt + 1) * 512],
                                hi[:, kc * 512 + nc0:kc * 512 + nc0 + 128],
                                xn[:, (8 + kc) * T + tt * 512:(8 + kc) * T + (tt + 1) * 512],
                                start=False, stop=(kc == 7)),
                                sm if (kc == 7) else None)
                i = ns * 4 + nsub
                ib = i % 2
                p.wait("scalar", sm)
                if i >= 2:
                    p.wait("vector", siP[ib])
                for tt in range(4):
                    p.wait("scalar", sv, p.val(sv))
                    p.op("scalar", lambda E, tt=tt: E.activation(
                        eg[:, :], psg[:, tt * 512:(tt + 1) * 512], AF.Copy), ss)
                    p.op("scalar", lambda E, tt=tt: E.activation(
                        e2[:, :], psg[:, tt * 512:(tt + 1) * 512], AF.Square), ss)
                    p.wait("vector", ss)
                    p.op("vector", lambda E: E.tensor_tensor(g3[:, :], e2[:, :], eg[:, :], ALU.mult), sv)
                    p.op("vector", lambda E: E.scalar_tensor_tensor(
                        g3[:, :], g3[:, :], 0.044715, eg[:, :], ALU.mult, ALU.add), sv)
                    p.wait("scalar", sv)
                    p.op("scalar", lambda E: E.activation(
                        th[:, :], g3[:, :], AF.Tanh, scale=0.7978845608028654), ss)
                    p.op("scalar", lambda E: E.activation(
                        tp1[:, :], th[:, :], AF.Identity, bias=ct[:, C_ONE:C_ONE + 1]), ss)
                    p.op("scalar", lambda E, tt=tt: E.activation(
                        ue[:, :], psu[:, tt * 512:(tt + 1) * 512], AF.Copy), ss)
                    p.wait("vector", ss)
                    p.op("vector", lambda E: E.tensor_tensor(g3[:, :], eg[:, :], tp1[:, :], ALU.mult), sv)
                    p.op("vector", lambda E, ib=ib, tt=tt: E.tensor_tensor(
                        intb[:, ib * T + tt * 512:ib * T + (tt + 1) * 512],
                        g3[:, :], ue[:, :], ALU.mult), sv)
                p.marks["gu_psum_free"] = p.val(sv)
                p.wait("sync", sv)
                p.dma("sync", inter_h[n0 + nc0:n0 + nc0 + 128, :],
                      intb[:, ib * T:(ib + 1) * T], siP[ib])
        for s_ in (so, sk0, sk1, sc0, sc1, st0, st1, si0, si1, sd, sg):
            p.wait("sync", s_)
        p.wait("sync", sg)
        block = es.enter_context(nc.Block())
        p.run(block)
    return nc



def build_ffn0b():
    nc = _mk_nc()
    inter_h = nc.dram_tensor("inter", [NF, T], BF16, kind="ExternalInput").ap()
    ddec_h = nc.dram_tensor("ddec", [NF, D], BF16, kind="ExternalInput").ap()
    px1_h = nc.dram_tensor("px1", [D, T], F32, kind="ExternalInput").ap()
    x1_h = nc.dram_tensor("x1", [D, T], F32, kind="ExternalOutput").ap()
    es = ExitStack()
    with es:
        intt = es.enter_context(nc.sbuf_tensor("intt", [128, 128 * 512], BF16))
        ddb = es.enter_context(nc.sbuf_tensor("ddb", [128, 2 * 1024], BF16))
        pxb = es.enter_context(nc.sbuf_tensor("pxb", [128, 512], F32))
        t1 = es.enter_context(nc.sbuf_tensor("t1", [128, 512], F32))
        x1b = es.enter_context(nc.sbuf_tensor("x1b", [128, 2 * 512], F32))
        ps0 = es.enter_context(nc.psum_tensor("ps0", [128, T], F32))
        ps1 = es.enter_context(nc.psum_tensor("ps1", [128, T], F32))
        S = _sems(es, nc, ["sd", "sg", "sg1", "so", "so1", "sdp", "sv", "ss", "sm", "se"])
        (sd, sg, sg1, so, so1, sdp, sv, ss, sm, se) = (
            S[k] for k in ["sd", "sg", "sg1", "so", "so1", "sdp", "sv", "ss", "sm", "se"])
        sgP, soP = [sg, sg1], [so, so1]
        p = Prog(nc)
        p.autosem = {"vector": sv, "scalar": ss}

        def psv(hc8):
            ps_ = ps0 if hc8 < 4 else ps1
            return ps_[:, (hc8 % 4) * 512:(hc8 % 4 + 1) * 512]

        for tt in range(4):
            p.wait("sync", sm)
            for half in range(2):
                p.dma("sync", intt[:, half * 64 * 512:(half + 1) * 64 * 512],
                      inter_h.rearrange("(ncc k) t -> k ncc t", k=128)
                      [:, half * 64:(half + 1) * 64, tt * 512:(tt + 1) * 512], sd)
            for hg in range(2):
                p.wait("tensor", ss, p.marks.get("d2_psum_free", 0))
                p.wait("tensor", sd)
                for ncc in range(128):
                    pb = ncc % 2
                    p.wait("gpsimd", sm, p.val(sm) - (1 if ncc >= 2 else 0))
                    p.dma("gpsimd", ddb[:, pb * 1024:(pb + 1) * 1024],
                          ddec_h[ncc * 128:(ncc + 1) * 128,
                                 hg * 1024:(hg + 1) * 1024], sgP[pb])
                    p.wait("tensor", sgP[pb])
                    for hc8 in range(8):
                        p.op("tensor", lambda E, pb=pb, hc8=hc8, ncc=ncc: E.matmul(
                            psv(hc8), ddb[:, pb * 1024 + hc8 * 128:pb * 1024 + (hc8 + 1) * 128],
                            intt[:, ncc * 512:(ncc + 1) * 512],
                            start=(ncc == 0), stop=(ncc == 127)),
                            sm if hc8 == 7 else None)
                p.wait("scalar", sm)
                for hc8 in range(8):
                    hc = hg * 8 + hc8
                    i = tt * 16 + hc
                    p.wait("sync", sv, p.val(sv))
                    p.dma("sync", pxb[:, :],
                          px1_h[hc * 128:(hc + 1) * 128, tt * 512:(tt + 1) * 512], sdp)
                    p.wait("scalar", sv, p.val(sv))
                    p.op("scalar", lambda E, hc8=hc8: E.activation(
                        t1[:, :], psv(hc8), AF.Copy), ss)
                    if hc8 == 7:
                        p.marks["d2_psum_free"] = p.val(ss)
                    p.wait("vector", ss)
                    p.wait("vector", sdp)
                    b = i % 2
                    if i >= 2:
                        p.wait("vector", soP[b])
                    p.op("vector", lambda E, b=b: E.tensor_tensor(
                        x1b[:, b * 512:(b + 1) * 512], t1[:, :], pxb[:, :], ALU.add), sv)
                    p.wait("sync", sv)
                    p.dma("sync", x1_h[hc * 128:(hc + 1) * 128, tt * 512:(tt + 1) * 512],
                          x1b[:, b * 512:(b + 1) * 512], soP[b])
        for s_ in (so, so1, sd, sg, sg1, sdp):
            p.wait("sync", s_)
        block = es.enter_context(nc.Block())
        p.run(block)
    return nc



def build_zmm():
    nc = _mk_nc()
    xn_h = nc.dram_tensor("xn2", [D, T], BF16, kind="ExternalInput").ap()
    wsh_h = nc.dram_tensor("wshT", [D, D], F32, kind="ExternalInput").ap()
    z_h = nc.dram_tensor("z", [D, T], BF16, kind="ExternalOutput").ap()
    es = ExitStack()
    with es:
        xn = es.enter_context(nc.sbuf_tensor("s_xn", [128, 16 * T], BF16))
        wb = es.enter_context(nc.sbuf_tensor("wb", [128, 2 * 2048], BF16))
        zb = es.enter_context(nc.sbuf_tensor("zb", [128, 2 * T], BF16))
        ps0 = es.enter_context(nc.psum_tensor("ps0", [128, T], F32))
        ps1 = es.enter_context(nc.psum_tensor("ps1", [128, T], F32))
        S = _sems(es, nc, ["sd", "sg", "sg1", "so", "so1", "sv", "ss", "sm", "se"])
        (sd, sg, sg1, so, so1, sv, ss, sm, se) = (
            S[k] for k in ["sd", "sg", "sg1", "so", "so1", "sv", "ss", "sm", "se"])
        sgP, soP = [sg, sg1], [so, so1]
        p = Prog(nc)
        p.autosem = {"vector": sv, "scalar": ss}
        for c in range(16):
            p.dma("sync", xn[:, c * T:(c + 1) * T], xn_h[c * 128:(c + 1) * 128, :], sd)
        for ec in range(16):
            eb = ec % 2
            ps_ = ps0 if eb == 0 else ps1
            p.wait("gpsimd", sm, max(0, p.val(sm) - 1))
            p.dma("gpsimd", wb[:, eb * 2048:(eb + 1) * 2048],
                  wsh_h.rearrange("(dc k) e -> k dc e", k=128)[:, :, ec * 128:(ec + 1) * 128], sgP[eb])
            p.wait("tensor", sd)
            p.wait("tensor", sgP[eb])
            if ec >= 2:
                p.wait("tensor", sv, p.marks.get(f"z_psum_free_{ec % 2}", 0))
            for tt in range(4):
                for dc in range(16):
                    p.op("tensor", lambda E, ps_=ps_, eb=eb, tt=tt, dc=dc: E.matmul(
                        ps_[:, tt * 512:(tt + 1) * 512],
                        wb[:, eb * 2048 + dc * 128:eb * 2048 + (dc + 1) * 128],
                        xn[:, dc * T + tt * 512:dc * T + (tt + 1) * 512],
                        start=(dc == 0), stop=(dc == 15)),
                        sm if (dc == 15 and tt == 3) else None)
            p.wait("vector", sm)
            if ec >= 2:
                p.wait("vector", soP[eb])
            p.op("vector", lambda E, ps_=ps_, eb=eb: E.tensor_copy(
                zb[:, eb * T:(eb + 1) * T], ps_[:, :]), sv)
            p.marks[f"z_psum_free_{eb}"] = p.val(sv)
            p.wait("sync", sv)
            p.dma("sync", z_h[ec * 128:(ec + 1) * 128, :], zb[:, eb * T:(eb + 1) * T], soP[eb])
        for s_ in (so, so1, sd, sg, sg1):
            p.wait("sync", s_)
        p.wait("sync", sg)
        block = es.enter_context(nc.Block())
        p.run(block)
    return nc



def build_l1attn():
    nc = _mk_nc()
    xn_h = nc.dram_tensor("xn2", [D, T], BF16, kind="ExternalInput").ap()
    zp_h = nc.dram_tensor("zp", [D, T], BF16, kind="ExternalInput").ap()
    gatec_h = nc.dram_tensor("gatec", [128, 16], F32, kind="ExternalInput").ap()
    lngc_h = nc.dram_tensor("lngc", [128, 16], F32, kind="ExternalInput").ap()
    lnbc_h = nc.dram_tensor("lnbc", [128, 16], F32, kind="ExternalInput").ap()
    wkvT_h = nc.dram_tensor("wkvT", [D, 128], F32, kind="ExternalInput").ap()
    tdc_h = nc.dram_tensor("tdc", [64, 1], F32, kind="ExternalInput").ap()
    tfc_h = nc.dram_tensor("tfc", [64, 1], F32, kind="ExternalInput").ap()
    consts = nc.dram_tensor("consts", [128, 8], F32, kind="ExternalInput").ap()
    onesc_h = nc.dram_tensor("onesc", [128, 1], F32, kind="ExternalInput").ap()
    onescb_h = nc.dram_tensor("onescb", [128, 1], BF16, kind="ExternalInput").ap()
    onesr_h = nc.dram_tensor("onesr", [1, 128], F32, kind="ExternalInput").ap()
    st_h = nc.dram_tensor("states", [64, T], F32, kind="ExternalOutput").ap()
    sf_h = nc.dram_tensor("sfin", [64, 1], F32, kind="ExternalOutput").ap()
    es = ExitStack()
    with es:
        xst = es.enter_context(nc.sbuf_tensor("xst", [128, 16 * T], BF16))
        xb = es.enter_context(nc.sbuf_tensor("xb", [128, T], BF16))
        zb = es.enter_context(nc.sbuf_tensor("zb", [128, T], BF16))
        t1 = es.enter_context(nc.sbuf_tensor("t1", [128, T], F32))
        sq = es.enter_context(nc.sbuf_tensor("sq", [128, T], F32))
        gt = es.enter_context(nc.sbuf_tensor("gt", [128, 16], F32))
        gmt = es.enter_context(nc.sbuf_tensor("gmt", [128, 16], F32))
        lng = es.enter_context(nc.sbuf_tensor("lng", [128, 16], F32))
        lnb = es.enter_context(nc.sbuf_tensor("lnb", [128, 16], F32))
        ct = es.enter_context(nc.sbuf_tensor("ct", [128, 8], F32))
        onesc = es.enter_context(nc.sbuf_tensor("s_onesc", [128, 1], F32))
        onescb = es.enter_context(nc.sbuf_tensor("s_onescb", [128, 1], BF16))
        onesr = es.enter_context(nc.sbuf_tensor("s_onesr", [1, 128], F32))
        srow = es.enter_context(nc.sbuf_tensor("srow", [1, 4 * T], F32))
        bc = es.enter_context(nc.sbuf_tensor("bc", [128, 2 * T], F32))
        wkv_w = es.enter_context(nc.sbuf_tensor("wkv_w", [128, 16 * 128], BF16))
        xnrm = es.enter_context(nc.sbuf_tensor("xnrm", [128, 16 * 512], BF16))
        kwk = es.enter_context(nc.sbuf_tensor("kwk", [64, 3 * 512], F32))
        wkvt = es.enter_context(nc.sbuf_tensor("wkvt", [64, T], F32))
        stt = es.enter_context(nc.sbuf_tensor("s_stt", [64, T], F32))
        tdc = es.enter_context(nc.sbuf_tensor("s_tdc", [64, 1], F32))
        negu = es.enter_context(nc.sbuf_tensor("negu", [64, 1], F32))
        wcol = es.enter_context(nc.sbuf_tensor("wcol", [64, 1], F32))
        ps = es.enter_context(nc.psum_tensor("ps", [1, T], F32))
        psb = es.enter_context(nc.psum_tensor("psb", [128, 512], F32))
        pskv = es.enter_context(nc.psum_tensor("pskv", [128, 512], F32))
        S = _sems(es, nc, ["sd", "sg", "so", "sv", "ss", "sm", "se"])
        sd, sg, so, sv, ss, sm, se = (S[k] for k in ["sd", "sg", "so", "sv", "ss", "sm", "se"])
        p = Prog(nc)
        p.autosem = {"vector": sv, "scalar": ss}

        p.dma("sync", ct[:, :], consts, sd)
        p.dma("sync", gt[:, :], gatec_h, sd)
        p.dma("sync", lng[:, :], lngc_h, sd)
        p.dma("sync", lnb[:, :], lnbc_h, sd)
        p.dma("sync", onesc[:, :], onesc_h, sd)
        p.dma("sync", onescb[:, :], onescb_h, sd)
        p.dma("sync", onesr[:, :], onesr_h, sd)
        p.dma("sync", tdc[:, :], tdc_h, sd)
        p.dma("sync", negu[:, :], tfc_h, sd)
        p.dma("gpsimd", wkv_w[:, :],
              wkvT_h.rearrange("(dc k) s -> k dc s", k=128), sg)
        p.wait("scalar", sd)
        p.wait("scalar", sg)
        p.op("scalar", lambda E: E.activation(gt[:, :], gt[:, :], AF.Sigmoid), ss)
        p.op("scalar", lambda E: E.activation(negu[:, :], negu[:, :], AF.Exp), ss)
        p.op("scalar", lambda E: E.activation(wcol[:, :], tdc[:, :], AF.Exp), ss)
        p.wait("vector", ss)
        p.op("vector", lambda E: E.tensor_scalar(gmt[:, :], gt[:, :], -1.0, 1.0, ALU.mult, ALU.add), sv)
        p.op("vector", lambda E: E.tensor_scalar(negu[:, :], negu[:, :], -1.0, None, ALU.mult), sv)
        p.wait("vector", sv)

        for c in range(16):
            p.wait("sync", sv, p.val(sv))
            p.dma("sync", xb[:, :], xn_h[c * 128:(c + 1) * 128, :], sd)
            p.dma("sync", zb[:, :], zp_h[c * 128:(c + 1) * 128, :], sd)
            p.wait("vector", sd)
            p.wait("vector", sg)
            p.op("vector", lambda E, c=c: E.tensor_scalar(
                sq[:, :], zb[:, :], gt[:, c:c + 1], None, ALU.mult), sv)
            p.op("vector", lambda E, c=c: E.tensor_scalar(
                t1[:, :], xb[:, :], gmt[:, c:c + 1], None, ALU.mult), sv)
            p.op("vector", lambda E, c=c: E.tensor_tensor(
                xst[:, c * T:(c + 1) * T], sq[:, :], t1[:, :], ALU.add), sv)
        p.wait("tensor", sv)
        for c in range(16):
            for tt in range(4):
                p.op("tensor", lambda E, c=c, tt=tt: E.matmul(
                    ps[0:1, tt * 512:(tt + 1) * 512], onescb[:, :],
                    xst[:, c * T + tt * 512:c * T + (tt + 1) * 512],
                    start=(c == 0), stop=(c == 15)),
                    sm if (c == 15 and tt == 3) else None)
        p.wait("vector", sm)
        MU, RS, NB, V1 = 0, 1, 2, 3
        p.op("vector", lambda E: E.tensor_scalar(
            srow[0:1, MU * T:(MU + 1) * T], ps[0:1, :], 1.0 / D, None, ALU.mult), sv)
        p.wait("scalar", sv)
        for c in range(16):
            p.wait("scalar", sm, p.val(sm))
            p.op("scalar", lambda E, c=c: E.activation(
                sq[:, :], xst[:, c * T:(c + 1) * T], AF.Square), ss)
            p.wait("tensor", ss)
            for tt in range(4):
                p.op("tensor", lambda E, c=c, tt=tt: E.matmul(
                    ps[0:1, tt * 512:(tt + 1) * 512], onesc[:, :],
                    sq[:, tt * 512:(tt + 1) * 512],
                    start=(c == 0), stop=(c == 15)),
                    sm if (tt == 3) else None)
        p.wait("vector", sm)
        p.op("vector", lambda E: E.tensor_scalar(
            srow[0:1, V1 * T:(V1 + 1) * T], ps[0:1, :], 1.0 / D, None, ALU.mult), sv)
        p.wait("vector", sv)
        p.op("vector", lambda E: E.tensor_tensor(
            srow[0:1, RS * T:(RS + 1) * T], srow[0:1, MU * T:(MU + 1) * T],
            srow[0:1, MU * T:(MU + 1) * T], ALU.mult), sv)
        p.wait("vector", sv)
        p.op("vector", lambda E: E.tensor_tensor(
            srow[0:1, V1 * T:(V1 + 1) * T], srow[0:1, V1 * T:(V1 + 1) * T],
            srow[0:1, RS * T:(RS + 1) * T], ALU.subtract), sv)
        p.wait("scalar", sv)
        p.op("scalar", lambda E: E.activation(
            srow[0:1, RS * T:(RS + 1) * T], srow[0:1, V1 * T:(V1 + 1) * T],
            AF.Ln, bias=ct[0:1, C_EPS:C_EPS + 1]), ss)
        p.wait("scalar", ss)
        p.op("scalar", lambda E: E.activation(
            srow[0:1, RS * T:(RS + 1) * T], srow[0:1, RS * T:(RS + 1) * T],
            AF.Exp, scale=-0.5), ss)
        p.wait("vector", ss)
        p.op("vector", lambda E: E.tensor_tensor(
            srow[0:1, NB * T:(NB + 1) * T], srow[0:1, MU * T:(MU + 1) * T],
            srow[0:1, RS * T:(RS + 1) * T], ALU.mult), sv)
        p.wait("vector", sv)
        p.op("vector", lambda E: E.tensor_scalar(
            srow[0:1, NB * T:(NB + 1) * T], srow[0:1, NB * T:(NB + 1) * T],
            -1.0, None, ALU.mult), sv)
        p.wait("vector", sv)
        p.wait("tensor", sv)
        for half, col in ((0, RS), (1, NB)):
            for tt in range(4):
                p.op("tensor", lambda E, col=col, tt=tt: E.matmul(
                    psb[:, :], onesr[:, :],
                    srow[0:1, col * T + tt * 512:col * T + (tt + 1) * 512],
                    start=True, stop=True), sm)
                p.wait("vector", sm)
                p.op("vector", lambda E, half=half, tt=tt: E.tensor_copy(
                    bc[:, half * T + tt * 512:half * T + (tt + 1) * 512],
                    psb[:, :]), sv)
                p.wait("tensor", sv)
        for tt in range(4):
            p.wait("vector", sm)
            for dc in range(16):
                p.op("vector", lambda E, tt=tt, dc=dc: E.tensor_tensor(
                    sq[:, 0:512], xst[:, dc * T + tt * 512:dc * T + (tt + 1) * 512],
                    bc[:, tt * 512:(tt + 1) * 512], ALU.mult), sv)
                p.op("vector", lambda E, tt=tt, dc=dc: E.tensor_tensor(
                    sq[:, 0:512], sq[:, 0:512],
                    bc[:, T + tt * 512:T + (tt + 1) * 512], ALU.add), sv)
                p.op("vector", lambda E, dc=dc: E.tensor_scalar(
                    xnrm[:, dc * 512:(dc + 1) * 512], sq[:, 0:512],
                    lng[:, dc:dc + 1], lnb[:, dc:dc + 1], ALU.mult, ALU.add), sv)
            p.wait("tensor", sv)
            p.wait("tensor", sg)
            p.wait("tensor", ss)
            for dc in range(16):
                p.op("tensor", lambda E, dc=dc: E.matmul(
                    pskv[:, :], wkv_w[:, dc * 128:(dc + 1) * 128],
                    xnrm[:, dc * 512:(dc + 1) * 512],
                    start=(dc == 0), stop=(dc == 15)),
                    sm if dc == 15 else None)
            p.wait("scalar", sm)
            p.op("scalar", lambda E: E.activation(kwk[0:64, 0:512], pskv[0:64, :], AF.Sigmoid), ss)
            p.wait("vector", ss)
            p.op("vector", lambda E: E.tensor_scalar(
                kwk[0:64, 512:1024], kwk[0:64, 0:512], negu[:, 0:1], None, ALU.mult), sv)
            p.wait("scalar", sv)
            p.op("scalar", lambda E: E.activation(kwk[0:64, 1024:1536], kwk[0:64, 512:1024], AF.Exp), ss)
            p.wait("vector", ss)
            p.op("vector", lambda E, tt=tt: E.tensor_tensor(
                wkvt[0:64, tt * 512:(tt + 1) * 512], kwk[0:64, 1024:1536],
                pskv[64:128, :], ALU.mult), sv, 1)
        p.wait("vector", sv)
        p.op("vector", lambda E: E.tensor_tensor_scan(
            stt[0:64, :], wcol[0:64, 0:1].broadcast_to([64, T]), wkvt[0:64, :],
            ct[0:64, C_ZERO:C_ZERO + 1], ALU.mult, ALU.add), sv)
        p.wait("sync", sv)
        p.dma("sync", st_h, stt[0:64, :], so)
        p.dma("sync", sf_h, stt[0:64, T - 1:T], so)
        p.wait("sync", so)
        p.wait("sync", sd)
        p.wait("sync", sg)
        block = es.enter_context(nc.Block())
        p.run(block)
    return nc



def build_l1proj():
    nc = _mk_nc()
    st_in = nc.dram_tensor("states", [64, T], F32, kind="ExternalInput").ap()
    sp_h = nc.dram_tensor("sp", [64, 1], F32, kind="ExternalInput").ap()
    sz_h = nc.dram_tensor("szero", [64, 1], F32, kind="ExternalInput").ap()
    tdr_h = nc.dram_tensor("tdrow", [1, 64], F32, kind="ExternalInput").ap()
    pos_h = nc.dram_tensor("posr", [1, T], F32, kind="ExternalInput").ap()
    wop_h = nc.dram_tensor("woutpT", [64, D], F32, kind="ExternalInput").ap()
    at_h = nc.dram_tensor("attn1", [D, T], F32, kind="ExternalOutput").ap()
    es = ExitStack()
    with es:
        stt = es.enter_context(nc.sbuf_tensor("s_stt", [64, T], F32))
        stb = es.enter_context(nc.sbuf_tensor("stb", [64, T], BF16))
        wp = es.enter_context(nc.sbuf_tensor("wp", [64, T], F32))
        spc = es.enter_context(nc.sbuf_tensor("spc", [64, 1], F32))
        szc = es.enter_context(nc.sbuf_tensor("szc", [64, 1], F32))
        tdr = es.enter_context(nc.sbuf_tensor("tdr", [1, 64], F32))
        posr = es.enter_context(nc.sbuf_tensor("s_posr", [1, T], F32))
        wob = es.enter_context(nc.sbuf_tensor("wob", [64, D], BF16))
        ob = es.enter_context(nc.sbuf_tensor("ob", [128, 2 * T], F32))
        psw = es.enter_context(nc.psum_tensor("psw", [64, T], F32))
        pso = es.enter_context(nc.psum_tensor("pso", [128, T], F32))
        S = _sems(es, nc, ["sd", "sg", "so", "so1", "sv", "ss", "sm", "se"])
        (sd, sg, so, so1, sv, ss, sm, se) = (
            S[k] for k in ["sd", "sg", "so", "so1", "sv", "ss", "sm", "se"])
        soP = [so, so1]
        p = Prog(nc)
        p.autosem = {"vector": sv, "scalar": ss}
        p.dma("sync", stt[0:64, :], st_in, sd)
        p.dma("sync", spc[0:64, :], sp_h, sd)
        p.dma("sync", szc[0:64, :], sz_h, sd)
        p.dma("sync", tdr[0:1, :], tdr_h, sd)
        p.dma("sync", posr[0:1, :], pos_h, sd)
        p.dma("gpsimd", wob[0:64, :], wop_h, sg)
        p.wait("tensor", sd)
        p.wait("tensor", sg)
        for tt in range(4):
            p.op("tensor", lambda E, tt=tt: E.matmul(
                psw[0:64, tt * 512:(tt + 1) * 512], tdr[0:1, :],
                posr[0:1, tt * 512:(tt + 1) * 512], start=True, stop=True),
                sm if tt == 3 else None)
        p.wait("scalar", sm)
        p.op("scalar", lambda E: E.activation(wp[0:64, :], psw[0:64, :], AF.Exp), ss)
        p.wait("vector", ss)
        p.op("vector", lambda E: E.tensor_tensor(spc[0:64, :], spc[0:64, :], szc[0:64, :], ALU.mult), sv)
        p.wait("vector", sv)
        p.op("vector", lambda E: E.tensor_scalar(wp[0:64, :], wp[0:64, :], spc[0:64, 0:1], None, ALU.mult), sv)
        p.op("vector", lambda E: E.tensor_tensor(stt[0:64, :], stt[0:64, :], wp[0:64, :], ALU.add), sv)
        p.wait("vector", sv)
        p.op("vector", lambda E: E.tensor_copy(stb[0:64, :], stt[0:64, :]), sv)
        p.wait("tensor", sv)
        for dc in range(16):
            for tt in range(4):
                p.op("tensor", lambda E, dc=dc, tt=tt: E.matmul(
                    pso[:, tt * 512:(tt + 1) * 512],
                    wob[0:64, dc * 128:(dc + 1) * 128],
                    stb[0:64, tt * 512:(tt + 1) * 512],
                    start=True, stop=True),
                    sm if tt == 3 else None)
            p.wait("vector", sm)
            db = dc % 2
            if dc >= 2:
                p.wait("vector", soP[db])
            p.op("vector", lambda E, db=db: E.tensor_copy(ob[:, db * T:(db + 1) * T], pso[:, :]), sv)
            p.wait("tensor", sv)
            p.wait("sync", sv)
            p.dma("sync", at_h[dc * 128:(dc + 1) * 128, :], ob[:, db * T:(db + 1) * T], soP[db])
        for s_ in (so, so1, sd, sg):
            p.wait("sync", s_)
        p.wait("sync", sg)
        block = es.enter_context(nc.Block())
        p.run(block)
    return nc



def build_ffn1final(debug=False):
    nc = _mk_nc()
    xn_h = nc.dram_tensor("xn2", [D, T], BF16, kind="ExternalInput").ap()
    w1_h = nc.dram_tensor("w1T", [D, H1], F32, kind="ExternalInput").ap()
    w2_h = nc.dram_tensor("w2T", [D, H1], F32, kind="ExternalInput").ap()
    w3_h = nc.dram_tensor("w3T", [H1, D], F32, kind="ExternalInput").ap()
    x1_h = nc.dram_tensor("x1", [D, T], F32, kind="ExternalInput").ap()
    at_h = nc.dram_tensor("attn1", [D, T], F32, kind="ExternalInput").ap()
    consts = nc.dram_tensor("consts", [128, 8], F32, kind="ExternalInput").ap()
    onesc_h = nc.dram_tensor("onesc", [128, 1], F32, kind="ExternalInput").ap()
    onesr_h = nc.dram_tensor("onesr", [1, 128], F32, kind="ExternalInput").ap()
    y_h = nc.dram_tensor("y", [D, T], F32, kind="ExternalOutput").ap()
    kind = "ExternalOutput" if debug else "Internal"
    hid_h = nc.dram_tensor("hidden", [H1, T], BF16, kind=kind)
    w3q_h = nc.dram_tensor("w3q", [H1, D], BF16, kind=kind)
    gdump_h = (nc.dram_tensor("gdump", [128, 4], F32, kind="ExternalOutput").ap()
               if debug else None)
    accdump_h = (nc.dram_tensor("accdump", [128, 4], F32, kind="ExternalOutput").ap()
                 if debug else None)
    sgmdump_h = (nc.dram_tensor("sgmdump", [128, 512], F32, kind="ExternalOutput").ap()
                 if debug else None)
    es = ExitStack()
    with es:
        xn = es.enter_context(nc.sbuf_tensor("s_xn", [128, 16 * T], BF16))
        ct = es.enter_context(nc.sbuf_tensor("ct", [128, 8], F32))
        onesc = es.enter_context(nc.sbuf_tensor("s_onesc", [128, 1], F32))
        onesr = es.enter_context(nc.sbuf_tensor("s_onesr", [1, 128], F32))
        arena = es.enter_context(nc.sbuf_tensor("arena", [128, 16384], I32))
        acc = es.enter_context(nc.sbuf_tensor("acc", [128, 4], F32))
        ginv = es.enter_context(nc.sbuf_tensor("ginv", [128, 4], F32))
        grow = es.enter_context(nc.sbuf_tensor("grow", [1, 4], F32))
        w1q = es.enter_context(nc.sbuf_tensor("w1q", [128, 16 * 512], BF16))
        w2q = es.enter_context(nc.sbuf_tensor("w2q", [128, 16 * 512], BF16))
        sgm = es.enter_context(nc.sbuf_tensor("sgm", [128, 512], F32))
        hb = es.enter_context(nc.sbuf_tensor("hb", [128, 2 * T], BF16))
        w3b = es.enter_context(nc.sbuf_tensor("w3b", [128, 44 * 128], BF16))
        x1b = es.enter_context(nc.sbuf_tensor("x1b", [128, 512], F32))
        atb = es.enter_context(nc.sbuf_tensor("atb", [128, 512], F32))
        yb = es.enter_context(nc.sbuf_tensor("yb", [128, 2 * 512], F32))
        t1 = es.enter_context(nc.sbuf_tensor("t1", [128, 512], F32))
        ps1 = es.enter_context(nc.psum_tensor("ps1", [128, T], F32))
        ps2 = es.enter_context(nc.psum_tensor("ps2", [128, T], F32))
        S = _sems(es, nc, ["sd", "sg", "so", "so1", "sdp", "sdh", "sv", "ss", "sm", "se"])
        (sd, sg, so, so1, sdp, sdh, sv, ss, sm, se) = (
            S[k] for k in ["sd", "sg", "so", "so1", "sdp", "sdh", "sv", "ss", "sm", "se"])
        soP = [so, so1]
        p = Prog(nc)
        p.autosem = {"vector": sv, "scalar": ss}
        wf = arena.ap().bitcast(F32)[:, 0:8192]
        qi = arena.ap()[:, 8192:16384]
        hidt = arena.ap().bitcast(BF16)[:, 0:22528]

        p.dma("sync", ct[:, :], consts, sd)
        p.dma("sync", onesc[:, :], onesc_h, sd)
        p.dma("sync", onesr[:, :], onesr_h, sd)
        for c in range(16):
            p.dma("sync", xn[:, c * T:(c + 1) * T], xn_h[c * 128:(c + 1) * 128, :], sd)

        for wi, (wh, nch, wid) in enumerate(((w1_h, 16, H1), (w2_h, 16, H1), (w3_h, 44, D))):
            for c in range(nch):
                p.wait("sync", sv, p.val(sv))
                p.dma("sync", wf[:, 0:wid], wh[c * 128:(c + 1) * 128, :], sd)
                p.wait("vector", sd)
                p.op("vector", lambda E, wid=wid, c=c: E.reduce_sum(
                    sgm[:, c:c + 1], wf[:, 0:wid], mybir.AxisListType.X,
                    apply_absolute_value=True), sv)
            p.wait("vector", sv)
            p.op("vector", lambda E, wi=wi, nch=nch: E.reduce_sum(
                acc[:, wi:wi + 1], sgm[:, 0:nch], mybir.AxisListType.X), sv)
            p.wait("vector", sv)
        p.wait("tensor", sv)
        p.op("tensor", lambda E: E.matmul(ps1[0:1, 0:3], onesc[:, :], acc[:, 0:3],
                                          start=True, stop=True), sm)
        p.wait("vector", sm)
        p.op("vector", lambda E: E.tensor_scalar(
            grow[0:1, 0:2], ps1[0:1, 0:2], 1.0 / (D * H1), 1e-8, ALU.mult, ALU.add), sv)
        p.op("vector", lambda E: E.tensor_scalar(
            grow[0:1, 2:3], ps1[0:1, 2:3], 1.0 / (D * H1), 1e-8, ALU.mult, ALU.add), sv)
        p.wait("vector", sv)
        p.op("vector", lambda E: E.reciprocal(grow[0:1, 0:3], grow[0:1, 0:3]), sv)
        p.wait("vector", sv)
        p.wait("tensor", sv)
        p.op("tensor", lambda E: E.matmul(ps2[:, 0:3], onesr[0:1, :], grow[0:1, 0:3],
                                          start=True, stop=True), sm)
        p.wait("vector", sm)
        p.op("vector", lambda E: E.tensor_copy(ginv[:, 0:3], ps2[:, 0:3]), sv)
        p.marks["f1_psum_free"] = p.val(sv)
        if debug:
            p.wait("sync", sv)
            p.dma("sync", gdump_h, ginv[:, :], so)
            p.dma("sync", accdump_h, acc[:, :], so)
            p.dma("sync", sgmdump_h, sgm[:, :], so)

        SO_G = p.val(S["so"])
        def quant(dst, src, gcol, n):
            p.op("vector", lambda E: E.tensor_scalar(
                qi[:, 0:n], src, ginv[:, gcol:gcol + 1], None, ALU.mult), sv)
            p.op("vector", lambda E: E.tensor_scalar(
                dst, qi[:, 0:n], -1.0, 1.0, ALU.max, ALU.min), sv)

        for c in range(44):
            p.wait("sync", sv, p.val(sv))
            p.dma("sync", wf[:, 0:D], w3_h[c * 128:(c + 1) * 128, :], sd)
            p.wait("vector", sd)
            if c >= 1:
                p.wait("vector", so)
            quant(hb[:, 0:D], wf[:, 0:D], 2, D)
            p.wait("sync", sv)
            p.dma("sync", w3q_h[c * 128:(c + 1) * 128, :], hb[:, 0:D], so)

        p.wait("vector", so)
        for hs in range(11):
            h0 = hs * 512
            for wi, (wh, wq) in enumerate(((w1_h, w1q), (w2_h, w2q))):
                p.wait("sync", sv, p.val(sv))
                p.dma("sync", wf[:, 0:16 * 512],
                      wh.rearrange("(dc k) h -> k dc h", k=128)[:, :, h0:h0 + 512], sd)
                p.wait("vector", sd)
                p.wait("vector", sg)
                p.wait("vector", sm)
                quant(wq[:, :], wf[:, 0:16 * 512], wi, 16 * 512)
            p.wait("tensor", sv)
            for hsub in range(4):
                hc0 = hsub * 128
                p.wait("tensor", sv, p.marks.get("f1_psum_free", 0))
                for ps_, wq in ((ps1, w1q), (ps2, w2q)):
                    for tt in range(4):
                        for dc in range(16):
                            p.op("tensor", lambda E, ps_=ps_, wq=wq, tt=tt, dc=dc, hc0=hc0: E.matmul(
                                ps_[:, tt * 512:(tt + 1) * 512],
                                wq[:, dc * 512 + hc0:dc * 512 + hc0 + 128],
                                xn[:, dc * T + tt * 512:dc * T + (tt + 1) * 512],
                                start=(dc == 0), stop=(dc == 15)),
                                sm if (dc == 15) else None)
                i = hs * 4 + hsub
                ib = i % 2
                p.wait("scalar", sm)
                if i >= 2:
                    p.wait("vector", soP[ib])
                for tt in range(4):
                    p.wait("scalar", sv, p.val(sv))
                    p.op("scalar", lambda E, tt=tt: E.activation(
                        sgm[:, :], ps1[:, tt * 512:(tt + 1) * 512], AF.Sigmoid), ss)
                    p.wait("vector", ss)
                    p.op("vector", lambda E, ib=ib, tt=tt: E.tensor_tensor(
                        hb[:, ib * T + tt * 512:ib * T + (tt + 1) * 512],
                        sgm[:, :], ps2[:, tt * 512:(tt + 1) * 512], ALU.mult), sv)
                p.marks["f1_psum_free"] = p.val(sv)
                p.wait("sync", sv)
                p.dma("sync", hid_h[h0 + hc0:h0 + hc0 + 128, :],
                      hb[:, ib * T:(ib + 1) * T], soP[ib])

        for s_ in (so, so1):
            p.wait("sync", s_)
            p.wait("gpsimd", s_)
        for tt in range(4):
            p.wait("sync", sm)
            p.dma("sync", hidt[:, :],
                  hid_h.rearrange("(hcc k) t -> k hcc t", k=128)[:, :, tt * 512:(tt + 1) * 512], sdh)
            for dc in range(16):
                i = tt * 16 + dc
                p.wait("gpsimd", sm)
                p.dma("gpsimd", w3b[:, :],
                      w3q_h.rearrange("(hcc k) d -> k hcc d", k=128)[:, :, dc * 128:(dc + 1) * 128], sg)
                p.wait("sync", sv, p.val(sv))
                p.dma("sync", x1b[:, :], x1_h[dc * 128:(dc + 1) * 128, tt * 512:(tt + 1) * 512], sdp)
                p.dma("sync", atb[:, :], at_h[dc * 128:(dc + 1) * 128, tt * 512:(tt + 1) * 512], sdp)
                p.wait("tensor", sdh)
                p.wait("tensor", sg)
                p.wait("tensor", ss, p.marks.get("f1d_psum_free", 0))
                for hcc in range(44):
                    p.op("tensor", lambda E, hcc=hcc: E.matmul(
                        ps1[:, 0:512], w3b[:, hcc * 128:(hcc + 1) * 128],
                        hidt[:, hcc * 512:(hcc + 1) * 512],
                        start=(hcc == 0), stop=(hcc == 43)),
                        sm if hcc == 43 else None)
                p.wait("scalar", sm)
                p.wait("scalar", sv, p.val(sv))
                p.op("scalar", lambda E: E.activation(t1[:, :], ps1[:, 0:512], AF.Copy), ss)
                p.marks["f1d_psum_free"] = p.val(ss)
                p.wait("vector", ss)
                p.wait("vector", sdp)
                b = i % 2
                if i >= 2:
                    p.wait("vector", soP[b])
                p.op("vector", lambda E, b=b: E.tensor_tensor(
                    yb[:, b * 512:(b + 1) * 512], t1[:, :], x1b[:, :], ALU.add), sv)
                p.op("vector", lambda E, b=b: E.tensor_tensor(
                    yb[:, b * 512:(b + 1) * 512], yb[:, b * 512:(b + 1) * 512],
                    atb[:, :], ALU.add), sv)
                p.wait("sync", sv)
                p.dma("sync", y_h[dc * 128:(dc + 1) * 128, tt * 512:(tt + 1) * 512],
                      yb[:, b * 512:(b + 1) * 512], soP[b])
        for s_ in (so, so1, sd, sg, sdp, sdh):
            p.wait("sync", s_)
        block = es.enter_context(nc.Block())
        p.run(block)
    return nc



_P = np.concatenate([np.arange(0, D, 2), np.arange(1, D, 2)])
_PINV = np.argsort(_P)
_R = np.concatenate([np.arange(0, 1024, 2), np.arange(1, 1024, 2), np.arange(1024, D)])
_NP = np.concatenate([np.arange(0, NF, 2), np.arange(1, NF, 2)])

_NC_CACHE = {}


def _nc(key, builder, *a, **kw):
    if key not in _NC_CACHE:
        _NC_CACHE[key] = builder(*a, **kw)
    return _NC_CACHE[key]


def _cc(x):
    return np.ascontiguousarray(x)


def kernel(L0_attn_X, L0_attn_W_in, L0_attn_b_in, L0_attn_W_out, L0_attn_b_out,
           L0_ffn_gate_W_packed, L0_ffn_gate_scales, L0_ffn_up_W_packed,
           L0_ffn_up_scales, L0_ffn_down_W_packed, L0_ffn_down_scales,
           L1_attn_W_key, L1_attn_W_value, L1_attn_W_output, L1_attn_W_shift,
           L1_attn_gate, L1_attn_time_decay, L1_attn_time_first,
           L1_attn_ln_gamma, L1_attn_ln_beta, L1_ffn_w1, L1_ffn_w2, L1_ffn_w3):
    f32 = np.float32
    X = np.asarray(L0_attn_X, f32)
    cores = list(range(8))

    rows_qkv = np.concatenate([_R, D + _R, 2 * D + _R])
    winT = _cc(np.asarray(L0_attn_W_in, f32)[np.ix_(rows_qkv, _P)].T)
    binc = _cc(np.asarray(L0_attn_b_in, f32)[rows_qkv].reshape(48, 128).T)
    woutT = _cc(np.asarray(L0_attn_W_out, f32)[np.ix_(_P, _R)].T)
    boutc = _cc(np.asarray(L0_attn_b_out, f32)[_P].reshape(16, 128).T)
    gpk = _cc(np.asarray(L0_ffn_gate_W_packed)[_NP].T.astype(np.int32))
    gsc = _cc(np.asarray(L0_ffn_gate_scales, f32)[_NP].T)
    upk = _cc(np.asarray(L0_ffn_up_W_packed)[_NP].T.astype(np.int32))
    usc = _cc(np.asarray(L0_ffn_up_scales, f32)[_NP].T)
    dpk = _cc(np.asarray(L0_ffn_down_W_packed)[_P].T.astype(np.int32))
    dsc = _cc(np.asarray(L0_ffn_down_scales, f32)[_P].T)
    wshT = _cc(np.asarray(L1_attn_W_shift, f32)[np.ix_(_P, _P)].T)
    gatec = _cc(np.asarray(L1_attn_gate, f32)[_P].reshape(16, 128).T)
    lngc = _cc(np.asarray(L1_attn_ln_gamma, f32)[_P].reshape(16, 128).T)
    lnbc = _cc(np.asarray(L1_attn_ln_beta, f32)[_P].reshape(16, 128).T)
    wkvT = _cc(np.concatenate([np.asarray(L1_attn_W_key, f32)[:, _P].T,
                               np.asarray(L1_attn_W_value, f32)[:, _P].T], axis=1))
    woutpT = _cc(np.asarray(L1_attn_W_output, f32)[_P].T)
    tdc = _cc(np.asarray(L1_attn_time_decay, f32).reshape(64, 1))
    tfc = _cc(np.asarray(L1_attn_time_first, f32).reshape(64, 1))
    tdrow = _cc(np.asarray(L1_attn_time_decay, f32).reshape(1, 64))
    w1T = _cc(np.asarray(L1_ffn_w1, f32)[:, _P].T)
    w2T = _cc(np.asarray(L1_ffn_w2, f32)[:, _P].T)
    w3T = _cc(np.asarray(L1_ffn_w3, f32)[_P].T)

    consts = np.zeros((128, 8), f32)
    consts[:, C_ONE] = 1.0
    consts[:, C_EPS] = 1e-5
    consts[:, C_75] = 7.5
    consts[:, C_KEPS] = 1e-24
    consts[:, C_EPS8] = 1e-8
    consts[:, C_HALF] = 0.5
    onesc = np.ones((128, 1), f32)
    onesr = np.ones((1, 128), f32)
    onescb = np.ones((128, 1), ml_dtypes.bfloat16)
    posr = np.arange(1, T + 1, dtype=f32).reshape(1, T)
    invf = (1.0 / 10000.0 ** (np.arange(0, 1024, 2, dtype=f32) / 1024.0)).astype(f32)
    cossin = []
    for half in range(2):
        pos = (half * T + np.arange(T, dtype=f32))
        fr = pos[None, :] * invf[:, None]
        cossin.append((np.cos(fr).astype(f32), np.sin(fr).astype(f32)))

    x0 = [_cc(X[c // 2, (c % 2) * T:(c % 2) * T + T, :].T[_P]) for c in cores]
    szero = [np.full((64, 1), float(c % 2), f32) for c in cores]

    def run(key, builder, per_core, *bargs):
        nc = _nc(key, builder, *bargs)
        res = run_bass_kernel_spmd(nc, per_core, core_ids=cores)
        return res.results

    r = run("ln", build_ln, [dict(xp0=x0[c], consts=consts, onesc=onesc,
                                  onesr=onesr) for c in cores], 1)
    xn = [r[c]["xn"] for c in cores]
    r = run("kva", build_kva, [dict(
        xn=xn[c], winT=winT, binc=binc, cosr=cossin[c % 2][0],
        sinr=cossin[c % 2][1], consts=consts, onesc=onesc, onesr=onesr)
        for c in cores])
    apart = [r[c]["apart"] for c in cores]
    r = run("qattn", build_qattn, [dict(
        xn=xn[c], x0=x0[c], winT=winT, binc=binc, cosr=cossin[c % 2][0],
        sinr=cossin[c % 2][1], woutT=woutT, boutc=boutc,
        amine=apart[c], apartner=apart[c ^ 1]) for c in cores])
    px1 = [r[c]["px1"] for c in cores]
    r = run("ffn0a", build_ffn0a, [dict(
        xn=xn[c], gpk=gpk, gsc=gsc, upk=upk, usc=usc, dpk=dpk, dsc=dsc,
        consts=consts) for c in cores])
    ddec = [r[c]["ddec"] for c in cores]
    inter = [r[c]["inter"] for c in cores]
    r = run("ffn0b", build_ffn0b, [dict(
        inter=inter[c], ddec=ddec[c], px1=px1[c]) for c in cores])
    x1 = [r[c]["x1"] for c in cores]
    r = run("ln", build_ln, [dict(xp0=x1[c], consts=consts, onesc=onesc,
                                  onesr=onesr) for c in cores], 1)
    xn2 = [r[c]["xn"] for c in cores]
    r = run("zmm", build_zmm, [dict(xn2=xn2[c], wshT=wshT) for c in cores])
    z = [r[c]["z"] for c in cores]
    r = run("l1attn", build_l1attn, [dict(
        xn2=xn2[c], zp=z[c ^ 1], gatec=gatec, lngc=lngc, lnbc=lnbc,
        wkvT=wkvT, tdc=tdc, tfc=tfc, consts=consts, onesc=onesc,
        onescb=onescb, onesr=onesr) for c in cores])
    states = [r[c]["states"] for c in cores]
    sfin = [r[c]["sfin"] for c in cores]
    r = run("l1proj", build_l1proj, [dict(
        states=states[c], sp=sfin[c ^ 1], szero=szero[c], tdrow=tdrow,
        posr=posr, woutpT=woutpT) for c in cores])
    attn1 = [r[c]["attn1"] for c in cores]
    r = run("ffn1final", build_ffn1final, [dict(
        xn2=xn2[c], w1T=w1T, w2T=w2T, w3T=w3T, x1=x1[c], attn1=attn1[c],
        consts=consts, onesc=onesc, onesr=onesr) for c in cores])
    y = [r[c]["y"] for c in cores]

    out = np.empty((4, 4096, D), f32)
    for c in cores:
        out[c // 2, (c % 2) * T:(c % 2) * T + T, :] = y[c][_PINV].T
    return out
